# version 16
# speedup vs baseline: 1.4061x; 1.0081x over previous
import math
from contextlib import ExitStack

import numpy as np

import concourse.bass as bass
import concourse.mybir as mybir
from concourse.bass_utils import run_bass_kernel_spmd

F32 = mybir.dt.float32
BF16 = mybir.dt.bfloat16
AF = mybir.ActivationFunctionType
ALU = mybir.AluOpType
AX = mybir.AxisListType

MAGIC = 12582912.0
NEGBIG = -1.0e9


class CFG:
    def __init__(self, D=2048, KV=4096, INNER=8192):
        self.D = D
        self.KV = KV
        self.OWN = KV // 2
        self.INNER = INNER
        self.H = D // 2
        self.DC = D // 128
        self.KT = KV // 128
        self.OT = self.OWN // 128
        self.OC = INNER // 128
        self.FB = 256
        self.NFB = self.OWN // self.FB
        self.NG = self.OT // 4
        self.EPS = 1e-5
        self.SCALE = 1.0 / math.sqrt(D)
        assert self.OWN % self.FB == 0 and self.OT % 4 == 0
        assert self.H % 128 == 0 and self.D % 256 == 0 and self.KV % 512 == 0


class Bld:
    def __init__(self, cfg: CFG, upto=99, qstage=3, pstage=3, astage=9):
        self.c = cfg
        self.upto = upto
        self.qstage = qstage
        self.pstage = pstage
        self.astage = astage
        self.nc = bass.Bass("TRN2", target_bir_lowering=False, debug=False)
        self.ctx = ExitStack()
        self.sems = {}
        self.cnt = {}
        self.bar_i = 0

    def sem(self, name):
        if name not in self.sems:
            self.sems[name] = self.nc.alloc_semaphore(name=name)
            self.cnt[name] = 0
        return self.sems[name]

    def release_prefix(self, *prefixes):
        names = [n for n in self.sems
                 if any(n.startswith(p) for p in prefixes) and not n.startswith("bar")]
        if not names:
            return
        handles = [self.sems[n] for n in names]

        def body(gpsimd):
            self.nc.clear_and_free_semaphores(handles)
        self.block.gpsimd(body)
        for n in names:
            del self.sems[n]
            del self.cnt[n]
        self.barrier()

    def inc(self, ins, name, k=1):
        ins.then_inc(self.sem(name), k)
        self.cnt[name] += k
        return self.cnt[name]

    def val(self, name):
        self.sem(name)
        return self.cnt[name]

    def wsnap(self, eng, name):
        eng.wait_ge(self.sem(name), self.val(name))

    def sbuf(self, name, shape, dtype):
        return self.ctx.enter_context(self.nc.sbuf_tensor(name, shape, dtype))

    def psum(self, name, shape, dtype=F32):
        return self.ctx.enter_context(self.nc.psum_tensor(name, shape, dtype))

    def barrier(self):
        name = f"bar{self.bar_i % 8}"
        self.bar_i += 1
        self.sem(name)
        target = self.cnt[name] + 5
        for e in ["tensor", "vector", "scalar", "gpsimd", "sync"]:
            def body(eng, name=name, target=target):
                eng.drain()
                eng.nop().then_inc(self.sems[name], 1)
                eng.wait_ge(self.sems[name], target)
            getattr(self.block, e)(body)
        self.cnt[name] = target

    def declare_dram(self):
        c, nc = self.c, self.nc
        D, KV, OWN, INNER, H = c.D, c.KV, c.OWN, c.INNER, c.H

        def din(name, shape):
            return nc.dram_tensor(name, shape, F32, kind="ExternalInput").ap()

        self.x_b = din("x_b", [KV, D])
        self.x_own = din("x_own", [OWN, D])
        self.wq = din("wq", [D, D])
        self.wk = din("wk", [D, D])
        self.wv = din("wv", [D, D])
        self.wo = din("wo", [D, D])
        self.w1t = din("w1t", [D, INNER])
        self.w2t = din("w2t", [INNER, D])
        self.b1r = din("b1r", [128, c.OC])
        self.gr = din("gr", [128, c.OC])
        self.brr = din("brr", [128, c.OC])
        self.b2r = din("b2r", [128, c.DC])
        self.cosk = din("cosk", [H, KV])
        self.sink = din("sink", [H, KV])
        self.cosq = din("cosq", [H, OWN])
        self.sinq = din("sinq", [H, OWN])
        self.maskp = din("maskp", [2 * 128, 128])
        self.ident_in = din("ident_in", [128, 128])

        self.y = nc.dram_tensor("y", [OWN, D], F32, kind="ExternalOutput").ap()

        def dscr(name, shape, dtype):
            return nc.dram_tensor(name, shape, dtype, kind="Internal").ap()

        self.kt_scr = dscr("kt_scr", [D, KV], BF16)
        self.v_scr = dscr("v_scr", [KV, D], BF16)
        self.qt_scr = dscr("qt_scr", [D, OWN], BF16)
        self.n_ocg = c.OC // 4
        self.w1q_scr = dscr("w1q_scr", [(c.OC // 2) * 128 * c.DC, 256], BF16)
        self.w2q_scr = dscr("w2q_scr", [c.DC * c.OC * 128, 128], BF16)
        self.base_scr = dscr("base_scr", [OWN, D], F32)

    def declare_persistent(self):
        c = self.c
        DC, OC, FB = c.DC, c.OC, c.FB
        self.ident_f = self.sbuf("ident_f", [128, 128], F32)
        self.mask_sb = self.sbuf("mask_sb", [128, 3 * 128], F32)
        self.b1r_sb = self.sbuf("b1r_sb", [128, OC], F32)
        self.gr_sb = self.sbuf("gr_sb", [128, OC], F32)
        self.brr_sb = self.sbuf("brr_sb", [128, OC], F32)
        self.b2r_sb = self.sbuf("b2r_sb", [128, DC], F32)
        self.statw_sb = self.sbuf("statw_sb", [128, 3 * OC], BF16)
        self.ones_row = self.sbuf("ones_row", [1, 128], F32)
        self.ones_colf = self.sbuf("ones_colf", [128, 1], F32)
        self.m1h_sb = self.sbuf("m1h_sb", [128, 1], F32)
        self.m2h_sb = self.sbuf("m2h_sb", [128, 1], F32)
        self.cst_sb = self.sbuf("cst_sb", [128, 4], F32)
        self.absacc = self.sbuf("absacc", [128, max(64, 3 * OC)], F32)
        self.red1 = self.sbuf("red1", [128, 1], F32)
        self.row4 = self.sbuf("row4", [1, 4], F32)
        self.ln_mu = self.sbuf("ln_mu", [128, 1], F32)
        self.ln_ssq = self.sbuf("ln_ssq", [128, 1], F32)
        self.ln_std = self.sbuf("ln_std", [128, 1], F32)
        self.ln_r = self.sbuf("ln_r", [128, 1], F32)
        self.eps_col = self.sbuf("eps_col", [128, 1], F32)
        self.sm = self.sbuf("sm", [128, 8], F32)
        self.fst = self.sbuf("fst", [128, 32], F32)
        self.srow = self.sbuf("srow", [1, 4 * FB], F32)
        self.stat_e = self.sbuf("stat_e", [4, FB], F32)
        self.stat_e2 = self.sbuf("stat_e2", [4, FB], F32)
        self.bca = self.sbuf("bca", [128, FB], F32)
        self.bcb = self.sbuf("bcb", [128, FB], F32)
        self.bcc = self.sbuf("bcc", [128, FB], F32)
        self.bcd = self.sbuf("bcd", [128, FB], F32)
        self.bce = self.sbuf("bce", [128, FB], F32)
        NKB = c.KT // 4
        self.NKB = NKB
        self.bm_sb = self.sbuf("bm_sb", [128, 4 * NKB], F32)
        self.lb_sb = self.sbuf("lb_sb", [128, 4 * NKB], F32)
        self.cc_sb = self.sbuf("cc_sb", [128, 4 * NKB], F32)
        self.ll_sb = self.sbuf("ll_sb", [128, 4], F32)
        self.rc_sb = self.sbuf("rc_sb", [128, 4 * NKB], F32)
        self.dtile = self.sbuf("dtile", [128, 128], BF16)
        self.PB = [self.psum(f"PB{i}", [128, 512], F32) for i in range(8)]

    def preduce_bcast(self, in_ap, out_ap, n, tag, ready_sem, ready_val):
        s, block = self, self.block
        c1, c2, c3, c4 = f"{tag}r1", f"{tag}r2", f"{tag}r3", f"{tag}r4"
        for nm in (c1, c2, c3, c4):
            s.sem(nm)

        @block.tensor
        def _(tensor):
            tensor.wait_ge(s.sem(ready_sem), ready_val)
            ins = tensor.matmul(s.PB[6][0:1, 0:n], s.ones_colf[:, 0:1], in_ap,
                                start=True, stop=True)
            s.inc(ins, c1, 1)
            tensor.wait_ge(s.sem(c2), s.val(c2) + 1)
            ins = tensor.matmul(s.PB[7][:, 0:n], s.ones_row[0:1, :], s.row4[0:1, 0:n],
                                start=True, stop=True)
            s.inc(ins, c3, 1)

        @block.scalar
        def _(scalar):
            scalar.wait_ge(s.sem(c1), s.val(c1))
            ins = scalar.copy(s.row4[0:1, 0:n], s.PB[6][0:1, 0:n])
            s.inc(ins, c2, 1)
            scalar.wait_ge(s.sem(c3), s.val(c3))
            ins = scalar.copy(out_ap, s.PB[7][:, 0:n])
            s.inc(ins, c4, 1)
        return c4

    def bcast_row(self, row_ap, out_ap, n, bank, tag, ready_sem, ready_val):
        s, block = self, self.block
        c1, c2 = f"{tag}b1", f"{tag}b2"
        s.sem(c1), s.sem(c2)

        @block.tensor
        def _(tensor):
            tensor.wait_ge(s.sem(ready_sem), ready_val)
            ins = tensor.matmul(s.PB[bank][:, 0:n], s.ones_row[0:1, :], row_ap,
                                start=True, stop=True)
            s.inc(ins, c1, 1)

        @block.vector
        def _(vector):
            vector.wait_ge(s.sem(c1), s.val(c1))
            ins = vector.tensor_copy(out_ap, s.PB[bank][:, 0:n])
            s.inc(ins, c2, 1)
        return c2

    def phase_init(self):
        s, block = self, self.block

        @block.sync
        def _(sync):
            s.inc(sync.dma_start(out=s.ident_f[:, :], in_=s.ident_in[:, :]), "d_init", 16)
            s.inc(sync.dma_start(out=s.mask_sb[:, 0:256].rearrange("p (a c) -> p a c", a=2),
                                 in_=s.maskp.rearrange("(a p) c -> p a c", p=128)), "d_init", 16)
            s.inc(sync.dma_start(out=s.b1r_sb[:, :], in_=s.b1r[:, :]), "d_init", 16)
            s.inc(sync.dma_start(out=s.gr_sb[:, :], in_=s.gr[:, :]), "d_init", 16)
            s.inc(sync.dma_start(out=s.brr_sb[:, :], in_=s.brr[:, :]), "d_init", 16)
            s.inc(sync.dma_start(out=s.b2r_sb[:, :], in_=s.b2r[:, :]), "d_init", 16)
            sync.wait_ge(s.sem("d_init"), s.val("d_init"))

        @block.vector
        def _(vector):
            vector.wait_ge(s.sem("d_init"), s.val("d_init"))
            OC = s.c.OC
            vector.memset(s.mask_sb[:, 256:384], NEGBIG)
            vector.memset(s.ones_row[:, :], 1.0)
            vector.memset(s.ones_colf[:, :], 1.0)
            vector.memset(s.eps_col[:, :], s.c.EPS)
            vector.memset(s.statw_sb[:, 0:OC], 1.0)
            vector.tensor_mul(s.statw_sb[:, OC:2 * OC], s.gr_sb[:, :], s.gr_sb[:, :])
            vector.tensor_mul(s.statw_sb[:, 2 * OC:3 * OC], s.gr_sb[:, :], s.brr_sb[:, :])
            vector.tensor_mul(s.absacc[:, 0:OC], s.gr_sb[:, :], s.gr_sb[:, :])
            vector.tensor_mul(s.absacc[:, OC:2 * OC], s.gr_sb[:, :], s.brr_sb[:, :])
            vector.tensor_mul(s.absacc[:, 2 * OC:3 * OC], s.brr_sb[:, :], s.brr_sb[:, :])
            vector.drain()
            vector.reduce_sum(s.cst_sb[:, 0:1], s.absacc[:, 0:OC], AX.X)
            vector.reduce_sum(s.cst_sb[:, 1:2], s.absacc[:, OC:2 * OC], AX.X)
            ins = vector.reduce_sum(s.cst_sb[:, 2:3], s.absacc[:, 2 * OC:3 * OC], AX.X)
            s.inc(ins, "c_init", 1)

        fin = self.preduce_bcast(s.cst_sb[:, 0:3], s.cst_sb[:, 0:3], 3, "CST", "c_init", 1)

        @block.vector
        def _(vector):
            vector.wait_ge(s.sem(fin), s.val(fin))
            ins = vector.tensor_scalar_mul(s.cst_sb[:, 3:4], s.cst_sb[:, 2:3],
                                           1.0 / s.c.INNER)
            s.inc(ins, "c_eb2", 1)

    def phase_q(self, WROWS, SGN, TQ):
        s, c, block = self, self.c, self.block
        jobs = [(0, s.w1t, c.D, c.INNER), (1, s.w2t, c.INNER, c.D)]
        items = []
        for wi, w_ap, rows, cols in jobs:
            cw = min(2048, cols)
            for blk in range(rows // 128):
                for pc in range(cols // cw):
                    items.append((wi, w_ap, blk, pc, rows, cols, cw))

        def w_write_ap(wi, blk, pc, cw):
            if wi == 0:
                v = s.w1q_scr.rearrange("(g p a) cc -> g a p cc", p=128, a=c.DC)
                g0, ng = pc * cw // 256, cw // 256
                return v[g0:g0 + ng, blk, :, :].rearrange("a p cc -> p a cc")
            assert pc == 0
            v = s.w2q_scr.rearrange("(x p a) cc -> x a p cc", p=128, a=c.OC // 2)
            hh, oo = blk // (c.OC // 2), blk % (c.OC // 2)
            return v[hh::2, oo, :, :].rearrange("a p cc -> p a cc")

        @block.sync
        def _(sync):
            for it in range(len(items)):
                if it >= 2:
                    sync.wait_ge(s.sem("cq_abs"), it - 1)
                wi, w_ap, blk, pc, rows, cols, cw = items[it]
                s.inc(sync.dma_start(out=WROWS[it % 2][:, 0:cw],
                                     in_=w_ap[blk * 128:(blk + 1) * 128, pc * cw:(pc + 1) * cw]),
                      f"dq_w{it % 2}", 16)

        @block.vector
        def _(vector):
            na, prev_wi = 0, 0
            for it in range(len(items)):
                wi, w_ap, blk, pc, rows, cols, cw = items[it]
                if wi != prev_wi:
                    na, prev_wi = 0, wi
                vector.wait_ge(s.sem(f"dq_w{it % 2}"), 16 * (it // 2 + 1))
                ins = vector.reduce_sum(s.absacc[:, na % 64:na % 64 + 1], WROWS[it % 2][:, 0:cw],
                                        AX.X, apply_absolute_value=True)
                s.inc(ins, "cq_abs", 1)
                na += 1
                last_of_w = (it + 1 == len(items)) or (items[it + 1][0] != wi)
                if na % 64 == 0 or last_of_w:
                    vector.drain()
                    k = na % 64 if na % 64 else 64
                    vector.reduce_sum(s.red1[:, 0:1], s.absacc[:, 0:k], AX.X)
                    vector.drain()
                    if na <= 64:
                        vector.tensor_copy(s.sm[:, wi:wi + 1], s.red1[:, 0:1])
                    else:
                        vector.tensor_add(s.sm[:, wi:wi + 1], s.sm[:, wi:wi + 1], s.red1[:, 0:1])
                    vector.drain()
            s.inc(vector.nop(), "rq", 1)

        if self.qstage < 2:
            return
        fin = self.preduce_bcast(s.sm[:, 0:2], s.sm[:, 0:2], 2, "QM", "rq", 1)

        @block.vector
        def _(vector):
            vector.wait_ge(s.sem(fin), s.val(fin))
            for wi, w_ap, rows, cols in jobs:
                dst = s.m1h_sb if wi == 0 else s.m2h_sb
                vector.tensor_scalar(dst[:, :], s.sm[:, wi:wi + 1], 1.0 / (rows * cols), 1e-5,
                                     ALU.mult, ALU.max)
                vector.drain()
                vector.tensor_scalar_mul(dst[:, :], dst[:, :], 0.5)
                vector.drain()
            vector.tensor_scalar_mul(s.sm[:, 2:3], s.m1h_sb[:, :], -1.0)
            ins = vector.tensor_scalar_mul(s.sm[:, 3:4], s.m2h_sb[:, :], -1.0)
            s.inc(ins, "mq", 1)

        if self.qstage < 3:
            return

        @block.sync
        def _(sync):
            def store(it):
                wi, w_ap, blk, pc, rows, cols, cw = items[it]
                sync.wait_ge(s.sem("cq_add"), it + 1)
                piece = 256 if wi == 0 else 128
                srcq = TQ[it % 2][:, 0:cw].rearrange("p (a cc) -> p a cc", cc=piece)
                s.inc(sync.dma_start(out=w_write_ap(wi, blk, pc, cw), in_=srcq),
                      f"dq_o{it % 2}", 16)

            sync.wait_ge(s.sem("cq_abs"), len(items))
            for it in range(len(items)):
                if it >= 2:
                    store(it - 2)
                    sync.wait_ge(s.sem("sq_sgn"), 2 * (it - 1))
                wi, w_ap, blk, pc, rows, cols, cw = items[it]
                s.inc(sync.dma_start(out=WROWS[it % 2][:, 0:cw],
                                     in_=w_ap[blk * 128:(blk + 1) * 128, pc * cw:(pc + 1) * cw]),
                      f"dq_x{it % 2}", 16)
            store(len(items) - 2)
            store(len(items) - 1)
            for nm in ("dq_o0", "dq_o1", "dq_w0", "dq_w1", "dq_x0", "dq_x1"):
                sync.wait_ge(s.sem(nm), s.val(nm))

        @block.scalar
        def _(scalar):
            scalar.wait_ge(s.sem("mq"), 1)
            for it in range(len(items)):
                wi, cw = items[it][0], items[it][6]
                mh = s.m1h_sb if wi == 0 else s.m2h_sb
                nmh = s.sm[:, 2:3] if wi == 0 else s.sm[:, 3:4]
                scalar.wait_ge(s.sem(f"dq_x{it % 2}"), 16 * (it // 2 + 1))
                if it >= 1:
                    scalar.wait_ge(s.sem("cq_add"), it)
                scalar.activation(SGN[0][:, 0:cw], WROWS[it % 2][:, 0:cw], AF.Sign, bias=nmh)
                ins = scalar.activation(SGN[1][:, 0:cw], WROWS[it % 2][:, 0:cw], AF.Sign,
                                        bias=mh[:, 0:1])
                s.inc(ins, "sq_sgn", 2)

        @block.vector
        def _(vector):
            for it in range(len(items)):
                cw = items[it][6]
                vector.wait_ge(s.sem("sq_sgn"), 2 * (it + 1))
                if it >= 2:
                    pv = it - 2
                    vector.wait_ge(s.sem(f"dq_o{pv % 2}"), 16 * (pv // 2 + 1))
                ins = vector.tensor_add(TQ[it % 2][:, 0:cw], SGN[0][:, 0:cw], SGN[1][:, 0:cw])
                s.inc(ins, "cq_add", 1)

    def emit_ln(self, x_src, ntiles, tag, bufs, pre=()):
        s, c, block = self, self.c, self.block
        D = c.D
        XA, XNF, SQD = bufs["XA"], bufs["XNF"], bufs["SQD"]
        dx, cmu, cms, cssq, cxn, cfree = (f"{tag}dx", f"{tag}mu", f"{tag}ms",
                                          f"{tag}ssq", f"{tag}xn", f"{tag}fr")
        for n in (dx, cmu, cms, cssq, cxn, cfree):
            s.sem(n)
        base = {n: s.val(n) for n in (dx, cmu, cms, cssq, cxn, cfree)}

        @block.sync
        def _(sync):
            for t in range(ntiles):
                if t >= 1:
                    sync.wait_ge(s.sem(cxn), base[cxn] + t)
                elif base[cxn]:
                    sync.wait_ge(s.sem(cxn), base[cxn])
                s.inc(sync.dma_start(out=XA[:, :], in_=x_src[t * 128:(t + 1) * 128, :]), dx, 16)
            sync.wait_ge(s.sem(dx), s.val(dx))

        @block.vector
        def _(vector):
            for t in range(ntiles):
                vector.wait_ge(s.sem(dx), base[dx] + 16 * (t + 1))
                if t >= 1:
                    vector.wait_ge(s.sem(cms), base[cms] + t)
                if t == 0:
                    for nm, v in pre:
                        vector.wait_ge(s.sem(nm), v)
                ins = vector.reduce_sum(s.ln_mu[:, :], XA[:, :], AX.X)
                s.inc(ins, cmu, 1)
                vector.wait_ge(s.sem(cssq), base[cssq] + t + 1)
                vector.reciprocal(s.ln_r[:, :], s.ln_std[:, :])
                vector.drain()
                if t >= 2:
                    vector.wait_ge(s.sem(cfree), base[cfree] + t - 1)
                ins = vector.tensor_scalar(XNF[t % 2][:, :], XA[:, :], s.ln_mu[:, 0:1],
                                           s.ln_r[:, 0:1], ALU.subtract, ALU.mult)
                s.inc(ins, cxn, 1)

        @block.scalar
        def _(scalar):
            for t in range(ntiles):
                scalar.wait_ge(s.sem(cmu), base[cmu] + t + 1)
                scalar.mul(s.ln_mu[:, :], s.ln_mu[:, :], 1.0 / D)
                scalar.drain()
                ins = scalar.activation(SQD[:, :], XA[:, :], AF.Square,
                                        bias=s.ln_mu[:, 0:1], scale=-1.0,
                                        accum_out=s.ln_ssq[:, :])
                s.inc(ins, cms, 1)
                scalar.drain()
                if t >= 1:
                    scalar.wait_ge(s.sem(cxn), base[cxn] + t)
                ins = scalar.activation(s.ln_std[:, :], s.ln_ssq[:, :], AF.Sqrt,
                                        bias=s.eps_col[:, 0:1], scale=1.0 / D)
                s.inc(ins, cssq, 1)

        return XNF, cxn, cfree, base

    def emit_transpose(self, srcs, cxn, cfree, cxn_base, ntiles, dst_cb, tag):
        s, c, block = self, self.c, self.block
        DC = c.DC
        ctp, cte = f"{tag}tp", f"{tag}te"
        s.sem(ctp), s.sem(cte)
        tb = {ctp: s.val(ctp), cte: s.val(cte)}
        ngroups = (DC + 3) // 4

        @block.tensor
        def _(tensor):
            for t in range(ntiles):
                tensor.wait_ge(s.sem(cxn), cxn_base + t + 1)
                if t >= 2:
                    tensor.wait_ge(s.sem(cte), tb[cte] + (t - 1) * DC)
                src = srcs[t % 2]
                for dc in range(DC):
                    bank = s.PB[(dc // 4) % 4 + (4 if t % 2 else 0)]
                    ins = tensor.transpose(bank[:, (dc % 4) * 128:(dc % 4) * 128 + 128],
                                           src[:, dc * 128:(dc + 1) * 128], s.ident_f[:, :])
                    if dc % 4 == 3 or dc == DC - 1:
                        s.inc(ins, ctp, 1)

        @block.scalar
        def _(scalar):
            ng = tb[ctp]
            for t in range(ntiles):
                ins = None
                for g in range(ngroups):
                    ng += 1
                    scalar.wait_ge(s.sem(ctp), ng)
                    for dc in range(g * 4, min(DC, g * 4 + 4)):
                        bank = s.PB[(dc // 4) % 4 + (4 if t % 2 else 0)]
                        ins = scalar.copy(dst_cb(t, dc),
                                          bank[:, (dc % 4) * 128:(dc % 4) * 128 + 128])
                    s.inc(ins, cte, min(DC, g * 4 + 4) - g * 4)

    def emit_lnt(self, x_src, ntiles, tag, bufs, dst_cb):
        s, c, block = self, self.c, self.block
        D, DC = c.D, c.DC
        XA, XNF, SQD = bufs["XA"], bufs["XNF"], bufs["SQD"]
        dx, cmu, cms, cssq, cxn, cfree = (f"{tag}dx", f"{tag}mu", f"{tag}ms",
                                          f"{tag}ssq", f"{tag}xn", f"{tag}fr")
        ctp, cte = f"{tag}tp", f"{tag}te"
        for n in (dx, cmu, cms, cssq, cxn, cfree, ctp, cte):
            s.sem(n)
        base = {n: s.val(n) for n in (dx, cmu, cms, cssq, cxn, cfree, ctp, cte)}
        ngroups = (DC + 3) // 4

        @block.sync
        def _(sync):
            for t in range(ntiles):
                if t >= 1:
                    sync.wait_ge(s.sem(cxn), base[cxn] + t)
                s.inc(sync.dma_start(out=XA[:, :], in_=x_src[t * 128:(t + 1) * 128, :]), dx, 16)
            sync.wait_ge(s.sem(dx), s.val(dx))

        @block.vector
        def _(vector):
            for t in range(ntiles):
                vector.wait_ge(s.sem(dx), base[dx] + 16 * (t + 1))
                if t >= 1:
                    vector.wait_ge(s.sem(cms), base[cms] + t)
                ins = vector.reduce_sum(s.ln_mu[:, :], XA[:, :], AX.X)
                s.inc(ins, cmu, 1)
                vector.wait_ge(s.sem(cssq), base[cssq] + t + 1)
                vector.reciprocal(s.ln_r[:, :], s.ln_std[:, :])
                vector.drain()
                if t >= 2:
                    vector.wait_ge(s.sem(cte), base[cte] + (t - 1) * DC)
                ins = vector.tensor_scalar(XNF[t % 2][:, :], XA[:, :], s.ln_mu[:, 0:1],
                                           s.ln_r[:, 0:1], ALU.subtract, ALU.mult)
                s.inc(ins, cxn, 1)

        @block.tensor
        def _(tensor):
            for t in range(ntiles):
                tensor.wait_ge(s.sem(cxn), base[cxn] + t + 1)
                if t >= 2:
                    tensor.wait_ge(s.sem(cte), base[cte] + (t - 1) * DC)
                src = XNF[t % 2]
                for dc in range(DC):
                    bank = s.PB[(dc // 4) % 4 + (4 if t % 2 else 0)]
                    ins = tensor.transpose(bank[:, (dc % 4) * 128:(dc % 4) * 128 + 128],
                                           src[:, dc * 128:(dc + 1) * 128], s.ident_f[:, :])
                    if dc % 4 == 3 or dc == DC - 1:
                        s.inc(ins, ctp, 1)

        @block.scalar
        def _(scalar):
            ng = base[ctp]

            def evac(t):
                nonlocal ng
                ins = None
                for g in range(ngroups):
                    ng += 1
                    scalar.wait_ge(s.sem(ctp), ng)
                    for dc in range(g * 4, min(DC, g * 4 + 4)):
                        bank = s.PB[(dc // 4) % 4 + (4 if t % 2 else 0)]
                        ins = scalar.copy(dst_cb(t, dc),
                                          bank[:, (dc % 4) * 128:(dc % 4) * 128 + 128])
                    s.inc(ins, cte, min(DC, g * 4 + 4) - g * 4)

            for t in range(ntiles):
                scalar.wait_ge(s.sem(cmu), base[cmu] + t + 1)
                scalar.mul(s.ln_mu[:, :], s.ln_mu[:, :], 1.0 / D)
                scalar.drain()
                ins = scalar.activation(SQD[:, :], XA[:, :], AF.Square,
                                        bias=s.ln_mu[:, 0:1], scale=-1.0,
                                        accum_out=s.ln_ssq[:, :])
                s.inc(ins, cms, 1)
                scalar.drain()
                if t >= 1:
                    scalar.wait_ge(s.sem(cxn), base[cxn] + t)
                ins = scalar.activation(s.ln_std[:, :], s.ln_ssq[:, :], AF.Sqrt,
                                        bias=s.eps_col[:, 0:1], scale=1.0 / D)
                s.inc(ins, cssq, 1)
                if t >= 1:
                    evac(t - 1)
            evac(ntiles - 1)

    def phase_a(self, bufs):
        s, c = self, self.c
        self.emit_lnt(s.x_b, c.KT, "A", bufs,
                      lambda t, dc: s.xnT[:, dc * c.KV + t * 128: dc * c.KV + t * 128 + 128])

    def emit_proj_rope(self, tag, w_ap, cos_ap, sin_ap, out_scr, tb_list, rhs_cb, bufs,
                       khraw, rhs_ready=None):
        s, c, block = self, self.c, self.block
        D, DC, HC = c.D, c.DC, c.H // 128
        WCB, TRIG, KTOUT, RT = bufs["WCB"], bufs["TRIG"], bufs["KTOUT"], bufs["RT"]
        dwc = [f"{tag}dw0", f"{tag}dw1"]
        dtr = [f"{tag}tr0", f"{tag}tr1"]
        dko = [f"{tag}ko0", f"{tag}ko1"]
        cpj, cpje, crp = f"{tag}pj", f"{tag}pe", f"{tag}rp"
        names = dwc + dtr + dko + [cpj, cpje, crp]
        for n in names:
            s.sem(n)
        base = {n: s.val(n) for n in names}

        @block.gpsimd
        def _(gpsimd):
            it = 0
            for tb in tb_list:
                for dco in range(DC):
                    if it >= 2:
                        gpsimd.wait_ge(s.sem(cpj), base[cpj] + it - 1)
                    src = w_ap[:, dco * 128:(dco + 1) * 128].rearrange("(a p) cc -> p a cc", p=128)
                    dst = WCB[it % 2].rearrange("p (a cc) -> p a cc", cc=128)
                    s.inc(gpsimd.dma_start(out=dst, in_=src), dwc[it % 2], 16)
                    it += 1
            gpsimd.wait_ge(s.sem(dwc[0]), s.val(dwc[0]))
            gpsimd.wait_ge(s.sem(dwc[1]), s.val(dwc[1]))

        @block.tensor
        def _(tensor):
            if rhs_ready is not None:
                tensor.wait_ge(s.sem(rhs_ready[0]), rhs_ready[1])
            it = 0
            for ti, tb in enumerate(tb_list):
                for dco in range(DC):
                    tensor.wait_ge(s.sem(dwc[it % 2]), base[dwc[it % 2]] + 16 * (it // 2 + 1))
                    if it >= 2:
                        tensor.wait_ge(s.sem(cpje), base[cpje] + it - 1)
                    bank = s.PB[it % 2]
                    ins = None
                    for dci in range(DC):
                        ins = tensor.matmul(bank[:, :], WCB[it % 2][:, dci * 128:(dci + 1) * 128],
                                            rhs_cb(tb, dci), start=(dci == 0), stop=(dci == DC - 1))
                    s.inc(ins, cpj, 1)
                    it += 1

        @block.scalar
        def _(scalar):
            it = 0
            for ti, tb in enumerate(tb_list):
                if ti >= 1 and self.pstage >= 2:
                    scalar.wait_ge(s.sem(crp), base[crp] + 2 * HC * ti)
                for dco in range(DC):
                    scalar.wait_ge(s.sem(cpj), base[cpj] + it + 1)
                    ins = scalar.copy(khraw[:, dco * 512:(dco + 1) * 512], s.PB[it % 2][:, :])
                    s.inc(ins, cpje, 1)
                    it += 1

        if self.pstage < 2:
            return

        jt_list = [(ti, tb, j) for ti, tb in enumerate(tb_list) for j in range(HC)]

        @block.sync
        def _(sync):
            def store(it2):
                ti2, tb2, j2 = jt_list[it2]
                sync.wait_ge(s.sem(crp), base[crp] + 2 * (it2 + 1))
                s.inc(sync.dma_start(out=out_scr[j2 * 128:(j2 + 1) * 128,
                                                 tb2 * 512:(tb2 + 1) * 512],
                                     in_=KTOUT[0][:, :]), dko[it2 % 2], 16)
                s.inc(sync.dma_start(out=out_scr[(j2 + HC) * 128:(j2 + HC + 1) * 128,
                                                 tb2 * 512:(tb2 + 1) * 512],
                                     in_=KTOUT[1][:, :]), dko[it2 % 2], 16)

            for it, (ti, tb, j) in enumerate(jt_list):
                if it >= 2:
                    sync.wait_ge(s.sem(crp), base[crp] + 2 * (it - 1))
                buf = TRIG[it % 2]
                s.inc(sync.dma_start(out=buf[:, 0:512],
                                     in_=cos_ap[j * 128:(j + 1) * 128, tb * 512:(tb + 1) * 512]),
                      dtr[it % 2], 16)
                s.inc(sync.dma_start(out=buf[:, 512:1024],
                                     in_=sin_ap[j * 128:(j + 1) * 128, tb * 512:(tb + 1) * 512]),
                      dtr[it % 2], 16)
                if self.pstage >= 3 and it >= 1:
                    store(it - 1)
            if self.pstage >= 3:
                store(len(jt_list) - 1)
                for nm in dko + dtr:
                    sync.wait_ge(s.sem(nm), s.val(nm))

        @block.vector
        def _(vector):
            it = 0
            for ti, tb in enumerate(tb_list):
                vector.wait_ge(s.sem(cpje), base[cpje] + (ti + 1) * DC)
                for j in range(HC):
                    vector.wait_ge(s.sem(dtr[it % 2]), base[dtr[it % 2]] + 32 * (it // 2 + 1))
                    if it >= 1 and self.pstage >= 3:
                        pv = it - 1
                        vector.wait_ge(s.sem(dko[pv % 2]), base[dko[pv % 2]] + 32 * (pv // 2 + 1))
                    cosb, sinb = TRIG[it % 2][:, 0:512], TRIG[it % 2][:, 512:1024]
                    k1 = khraw[:, j * 512:(j + 1) * 512]
                    k2 = khraw[:, (j + HC) * 512:(j + HC + 1) * 512]
                    vector.tensor_mul(RT[0][:, :], k1, cosb)
                    vector.tensor_mul(RT[1][:, :], k2, sinb)
                    vector.drain()
                    ins = vector.tensor_sub(KTOUT[0][:, :], RT[0][:, :], RT[1][:, :])
                    s.inc(ins, crp, 1)
                    vector.drain()
                    vector.tensor_mul(RT[0][:, :], k1, sinb)
                    vector.tensor_mul(RT[1][:, :], k2, cosb)
                    vector.drain()
                    ins = vector.tensor_add(KTOUT[1][:, :], RT[0][:, :], RT[1][:, :])
                    s.inc(ins, crp, 1)
                    vector.drain()
                    it += 1


    def phase_b1(self, bufs, khraw):
        s, c = self, self.c
        self.emit_proj_rope("K", s.wk, s.cosk, s.sink, s.kt_scr, list(range(c.KV // 512)),
                            lambda tb, dci: s.xnT[:, dci * c.KV + tb * 512: dci * c.KV + tb * 512 + 512],
                            bufs, khraw)

    def phase_b2(self, bufs, wvh):
        s, c, block = self, self.c, self.block
        D, DC, KT = c.D, c.DC, c.KT
        KTOUT = bufs["KTOUT"]
        VW = min(512, D // 2)
        nitems = []
        for half in range(2):
            for kt in range(KT):
                for do in range((D // 2) // VW):
                    nitems.append((half, kt, do))
        per_half = KT * ((D // 2) // VW)

        @block.gpsimd
        def _(gpsimd):
            for half in range(2):
                if half == 1:
                    gpsimd.wait_ge(s.sem("Vpj"), per_half)
                src = s.wv[:, half * (D // 2):(half + 1) * (D // 2)].rearrange(
                    "(a p) cc -> p a cc", p=128)
                dst = wvh[:, :].rearrange("p (a cc) -> p a cc", cc=D // 2)
                s.inc(gpsimd.dma_start(out=dst, in_=src), f"Vdw{half}", 16)
            gpsimd.wait_ge(s.sem("Vdw0"), s.val("Vdw0"))
            gpsimd.wait_ge(s.sem("Vdw1"), s.val("Vdw1"))

        @block.tensor
        def _(tensor):
            for it, (half, kt, do) in enumerate(nitems):
                tensor.wait_ge(s.sem(f"Vdw{half}"), 16)
                if it >= 2:
                    tensor.wait_ge(s.sem("Vpe"), it - 1)
                bank = s.PB[it % 2]
                ins = None
                for dci in range(DC):
                    ins = tensor.matmul(
                        bank[:, 0:VW],
                        s.xnT[:, dci * c.KV + kt * 128: dci * c.KV + kt * 128 + 128],
                        wvh[:, dci * (D // 2) + do * VW: dci * (D // 2) + do * VW + VW],
                        start=(dci == 0), stop=(dci == DC - 1))
                s.inc(ins, "Vpj", 1)

        @block.scalar
        def _(scalar):
            for it in range(len(nitems)):
                scalar.wait_ge(s.sem("Vpj"), it + 1)
                if it >= 2:
                    pv = it - 2
                    scalar.wait_ge(s.sem(f"Vst{pv % 2}"), 16 * (pv // 2 + 1))
                ins = scalar.copy(KTOUT[it % 2][:, 0:VW], s.PB[it % 2][:, 0:VW])
                s.inc(ins, "Vpe", 1)

        @block.sync
        def _(sync):
            for it, (half, kt, do) in enumerate(nitems):
                sync.wait_ge(s.sem("Vpe"), it + 1)
                col = half * (D // 2) + do * VW
                s.inc(sync.dma_start(out=s.v_scr[kt * 128:(kt + 1) * 128, col:col + VW],
                                     in_=KTOUT[it % 2][:, 0:VW]), f"Vst{it % 2}", 16)
            sync.wait_ge(s.sem("Vst0"), s.val("Vst0"))
            sync.wait_ge(s.sem("Vst1"), s.val("Vst1"))

    def phase_b3(self, bufs, khraw, xnqT):
        s, c = self, self.c
        for qb in range(c.OWN // 512):
            self.emit_lnt(s.x_own[qb * 512:(qb + 1) * 512, :], 4, f"Q{qb}L", bufs,
                          lambda t, dc: xnqT[:, dc * 512 + t * 128: dc * 512 + t * 128 + 128])
            self.emit_proj_rope(f"Q{qb}P", s.wq, s.cosq, s.sinq, s.qt_scr, [qb],
                                lambda tb, dci: xnqT[:, dci * 512: dci * 512 + 512],
                                bufs, khraw,
                                rhs_ready=(f"Q{qb}Lte", self.val(f"Q{qb}Lte")))
            if qb + 1 < c.OWN // 512:
                self.barrier()
                self.release_prefix(f"Q{qb}")

    def phase_f(self, b):
        s, c, block = self, self.c, self.block
        D, DC, OC, FB, INNER = c.D, c.DC, c.OC, c.FB, c.INNER
        NT = FB // 128
        H, Q2, XQT = b["H"], b["Q2"], b["XQT"]

        csx, csd, cq1 = "F_sx", "F_sd", "F_q1"
        ctt, cte2 = "F_ct", "F_ce"
        cst_l = ["F_s0", "F_s1"]
        MBK = [0, 1, 4, 5]
        cm1, cm1e = "F_m1", "F_me"
        ctg, cgl, chz, chh, chb, chs, cstm = ("F_tg", "F_gl", "F_hz", "F_hh",
                                              "F_hb", "F_hs", "F_sm")
        cse, cstt, ctok = "F_se", "F_stt", "F_tk"
        crr, cpk, crr2, cpk2 = "F_rr", "F_pk", "F_r2", "F_p2"
        ctr2, cte3, ctr3, cte5 = "F_t2", "F_e3", "F_t3", "F_e5"
        camx, ctm, ctk2, cq2 = "F_am", "F_tm", "F_k2", "F_q2"
        cw2 = ["F_w0", "F_w1"]
        cm2, cm2e, cffn = "F_m2", "F_m2e", "F_ff"
        cbt, cbe = "F_bt", "F_be"
        REL = [csx, csd, cq1, "F_Ttp", "F_Tte", ctt, cst_l[0], cst_l[1], cm1, cm1e,
               ctg, cgl, "F_hp",
               chz, chh, chb, chs, cstm, cse, ctk2, cq2, cw2[0], cw2[1], cm2,
               cm2e, cffn, cbt, cbe, "F_bs0", "F_bs1"]
        for nm in REL + [cte2, cstt, ctok, crr, cpk, crr2, cpk2, ctr2, cte3,
                         ctr3, cte5, camx, ctm, "F_bx0", "F_bx1"]:
            s.sem(nm)
        n_ocg = OC // 4
        OH = OC // 2
        XPO = max(1, D // FB)

        def emit_b_w2(pb, fb2):
            @block.sync
            def _(sync):
                it = 0
                for dc in range(DC):
                    for hh in range(2):
                        if it >= 2:
                            sync.wait_ge(s.sem(cm2), fb2[cm2] + it - 1)
                        elif pb > 0:
                            sync.wait_ge(s.sem(cm2), fb2[cm2])
                        r0 = (dc * 2 + hh) * 128 * OH
                        src = s.w2q_scr[r0:r0 + OH * 128, :].rearrange(
                            "(p a) cc -> p (a cc)", p=128)
                        s.inc(sync.dma_start(out=b["W2"][it % 2][:, :], in_=src),
                              cw2[it % 2], 16)
                        it += 1

        def emit_b_mm2(pb, fb2, cse_tgt):
            @block.tensor
            def _(tensor):
                tensor.wait_ge(s.sem(cq2), fb2[cq2])
                tensor.wait_ge(s.sem(cse), cse_tgt)
                tensor.wait_ge(s.sem(cm2e), fb2[cm2e])
                it = 0
                for dc in range(DC):
                    bank = s.PB[2 + dc % 2]
                    if dc >= 2:
                        tensor.wait_ge(s.sem(cm2e), fb2[cm2e] + dc - 1)
                    for hh in range(2):
                        tensor.wait_ge(s.sem(cw2[it % 2]),
                                       fb2[cw2[it % 2]] + 16 * (it // 2 + 1))
                        for oo in range(OH):
                            oc = hh * OH + oo
                            ins = tensor.matmul(bank[:, 0:FB],
                                                b["W2"][it % 2][:, oo * 128:(oo + 1) * 128],
                                                Q2[:, oc * FB:(oc + 1) * FB],
                                                start=(oc == 0), stop=(oc == OC - 1))
                        s.inc(ins, cm2, 1)
                        it += 1

        def evac_mm2(vector, pb, fb2, dc):
            vector.wait_ge(s.sem(cm2), fb2[cm2] + 2 * (dc + 1))
            if dc >= 2:
                vector.wait_ge(s.sem(cffn), fb2[cffn] + dc - 1)
            elif pb > 0:
                vector.wait_ge(s.sem(cffn), fb2[cffn])
            ins = vector.tensor_mul(b["TF2"][dc % 2][:, :], s.PB[2 + dc % 2][:, 0:FB],
                                    s.bce[:, :])
            s.inc(ins, cm2e, 1)

        def emit_b_ffn(pb, fb2):
            @block.scalar
            def _(scalar):
                scalar.wait_ge(s.sem(cbt), fb2[cbt])
                for dc in range(DC):
                    scalar.wait_ge(s.sem(cm2e), fb2[cm2e] + dc + 1)
                    ins = scalar.activation(b["FT"][:, dc * FB:(dc + 1) * FB],
                                            b["TF2"][dc % 2][:, :],
                                            AF.Identity, bias=s.b2r_sb[:, dc:dc + 1])
                    s.inc(ins, cffn, 1)

        def emit_b_base(pb, fb2):
            p0 = pb * FB
            HD = DC // 2
            DH = D // 2
            XNF = [b["XNF0"], b["XNF1"]]

            @block.sync
            def _(sync):
                for t in range(NT):
                    if t >= 1:
                        sync.wait_ge(s.sem(cbe), fb2[cbe] + t * DC)
                    sync.wait_ge(s.sem(csx), s.val(csx))
                    sync.wait_ge(s.sem("F_Ttp"), s.val("F_Ttp"))
                    s.inc(sync.dma_start(out=XNF[t % 2][:, :],
                                         in_=s.x_own[p0 + t * 128:p0 + (t + 1) * 128, :]),
                          f"F_bx{t % 2}", 16)
                    for h in range(2):
                        sync.wait_ge(s.sem(cbe), fb2[cbe] + t * DC + (h + 1) * HD)
                        s.inc(sync.dma_start(
                            out=s.base_scr[p0 + t * 128:p0 + (t + 1) * 128,
                                           h * DH:(h + 1) * DH],
                            in_=b["BB"][:, :]), f"F_bs{(2 * t + h) % 2}", 16)
                if pb == c.NFB - 1:
                    for t in range(NT):
                        sync.wait_ge(s.sem(f"F_bs{t % 2}"), s.val(f"F_bs{t % 2}"))

            @block.tensor
            def _(tensor):
                tensor.wait_ge(s.sem(cffn), fb2[cffn] + DC)
                tensor.wait_ge(s.sem(ctk2), s.val(ctk2))
                n = 0
                for t in range(NT):
                    for dc in range(DC):
                        bank = s.PB[4 + (n % 2)]
                        if n >= 2:
                            tensor.wait_ge(s.sem(cbe), fb2[cbe] + n - 1)
                        ins = tensor.transpose(
                            bank[:, 0:128],
                            b["FT"][:, dc * FB + t * 128: dc * FB + t * 128 + 128],
                            s.ident_f[:, :])
                        s.inc(ins, cbt, 1)
                        n += 1

            @block.vector
            def _(vector):
                vector.wait_ge(s.sem("F_bs0"), fb2["F_bs0"])
                vector.wait_ge(s.sem("F_bs1"), fb2["F_bs1"])
                n = 0
                for t in range(NT):
                    vector.wait_ge(s.sem(f"F_bx{t % 2}"), s.val(f"F_bx{t % 2}"))
                    for dc in range(DC):
                        if dc % HD == 0 and (t or dc):
                            pidx = 2 * t + dc // HD - 1
                            vector.wait_ge(s.sem(f"F_bs{pidx % 2}"),
                                           fb2[f"F_bs{pidx % 2}"] + 16 * (pidx // 2 + 1))
                        vector.wait_ge(s.sem(cbt), fb2[cbt] + n + 1)
                        bank = s.PB[4 + (n % 2)]
                        ins = vector.tensor_add(b["BB"][:, (dc % HD) * 128:
                                                        (dc % HD + 1) * 128],
                                                bank[:, 0:128],
                                                XNF[t % 2][:, dc * 128:(dc + 1) * 128])
                        s.inc(ins, cbe, 1)
                        vector.drain()
                        n += 1

        def emit_a1(fbk):
            t0k = fbk * FB
            a1B = {nm: s.val(nm) for nm in (csd, ctt, "F_Ttp", cq1, csx)}
            XNF, cxn, cfree, base = self.emit_ln(s.x_own[t0k:t0k + FB, :], NT, "F_L", b,
                                                 pre=[(csx, a1B[csx]),
                                                      ("F_Ttp", a1B["F_Ttp"])])
            q1base = a1B[cq1]

            @block.scalar
            def _(scalar):
                for t in range(NT):
                    scalar.wait_ge(s.sem(cxn), base[cxn] + t + 1)
                    ins = scalar.activation(b["SQD"][:, :], XNF[t % 2][:, :], AF.Square,
                                            accum_out=s.fst[:, 16 + t:17 + t])
                    s.inc(ins, csx, 1)
                    scalar.drain()
                    ins = scalar.activation(s.fst[:, 18 + t:19 + t], s.fst[:, 16 + t:17 + t],
                                            AF.Sqrt, bias=s.eps_col[:, 0:1], scale=1.0 / D)
                    s.inc(ins, csd, 1)

            @block.vector
            def _(vector):
                vector.wait_ge(s.sem(ctt), a1B[ctt])
                for t in range(NT):
                    vector.wait_ge(s.sem(csd), a1B[csd] + t + 1)
                    vector.reciprocal(s.fst[:, 0:1], s.fst[:, 18 + t:19 + t])
                    vector.drain()
                    vector.reduce_max(s.fst[:, 1:2], XNF[t % 2][:, :], AX.X,
                                      apply_absolute_value=True)
                    vector.drain()
                    vector.tensor_mul(s.fst[:, 2:3], s.fst[:, 0:1], s.fst[:, 1:2])
                    vector.drain()
                    vector.tensor_scalar_max(s.fst[:, 2:3], s.fst[:, 2:3], 1e-5)
                    vector.drain()
                    vector.reciprocal(s.fst[:, 3:4], s.fst[:, 2:3])
                    vector.drain()
                    vector.tensor_mul(s.fst[:, 3:4], s.fst[:, 3:4], s.fst[:, 0:1])
                    vector.drain()
                    vector.tensor_scalar_mul(s.fst[:, 3:4], s.fst[:, 3:4], 127.0)
                    vector.drain()
                    vector.tensor_mul(s.fst[:, 20 + t:21 + t], s.fst[:, 2:3], s.m1h_sb[:, 0:1])
                    vector.drain()
                    vector.tensor_scalar_mul(s.fst[:, 20 + t:21 + t], s.fst[:, 20 + t:21 + t],
                                             1.0 / 127.0)
                    vector.drain()
                    if t == 0:
                        vector.wait_ge(s.sem("F_Ttp"), a1B["F_Ttp"])
                    vector.tensor_scalar(b["Q1"][t % 2][:, :], XNF[t % 2][:, :],
                                         s.fst[:, 3:4], MAGIC, ALU.mult, ALU.add)
                    vector.drain()
                    ins = vector.tensor_scalar(b["Q1"][t % 2][:, :], b["Q1"][t % 2][:, :],
                                               MAGIC, None, ALU.subtract)
                    s.inc(ins, cq1, 1)

            self.emit_transpose(b["Q1"], cq1, None, q1base, NT,
                                lambda t, dc: XQT[:, dc * FB + t * 128: dc * FB + t * 128 + 128],
                                "F_T")

            @block.tensor
            def _(tensor):
                tensor.wait_ge(s.sem(cq1), q1base + NT)
                tensor.wait_ge(s.sem("F_Tte"), s.val("F_Tte"))
                ins = None
                for t in range(NT):
                    ins = tensor.transpose(s.PB[6][0:1, t * 128:t * 128 + 128],
                                           s.fst[:, 20 + t:21 + t], s.ident_f[:, :])
                s.inc(ins, ctt, 1)

            @block.scalar
            def _(scalar):
                scalar.wait_ge(s.sem(ctt), s.val(ctt))
                ins = scalar.copy(s.srow[0:1, 0:FB], s.PB[6][0:1, 0:FB])
                s.inc(ins, cte2, 1)

            self.bcast_row(s.srow[0:1, 0:FB], s.bcd[:, :], FB, 7, "F_cs",
                           cte2, s.val(cte2))

        emit_a1(0)
        for fb in range(c.NFB):
            t0 = fb * FB
            pb = fb - 1
            fB = {nm: s.val(nm) for nm in REL}

            @block.sync
            def _(sync):
                for g in range(OC // 2):
                    if g >= 2:
                        sync.wait_ge(s.sem(cm1), fB[cm1] + 2 * (g - 1))
                    elif fb > 0:
                        sync.wait_ge(s.sem(cm1), fB[cm1])
                    src = s.w1q_scr[g * 128 * DC:(g + 1) * 128 * DC, :].rearrange(
                        "(p a) cc -> p (a cc)", p=128)
                    s.inc(sync.dma_start(out=b["ST"][g % 2][:, :], in_=src),
                          cst_l[g % 2], 16)

            def stat_mm(tensor, oc):
                tensor.wait_ge(s.sem(chb), fB[chb] + oc + 1)
                tensor.wait_ge(s.sem(chs), fB[chs] + oc + 1)
                lw3 = s.statw_sb[:, oc:2 * OC + oc + 1:OC]
                tensor.matmul(s.PB[2][0:3, 0:FB], lw3, b["HB"][oc % 4][:, :],
                              start=(oc == 0), stop=(oc == OC - 1))
                lw2 = s.statw_sb[:, oc:OC + oc + 1:OC]
                ins2 = tensor.matmul(s.PB[3][0:2, 0:FB], lw2, b["HS"][oc % 4][:, :],
                                     start=(oc == 0), stop=(oc == OC - 1))
                s.inc(ins2, cstm, 2)

            @block.tensor
            def _(tensor):
                tensor.wait_ge(s.sem("F_Tte"), s.val("F_Tte"))
                for g in range(OC // 2):
                    tensor.wait_ge(s.sem(cst_l[g % 2]), fB[cst_l[g % 2]] + 16 * (g // 2 + 1))
                    for o2 in range(2):
                        oc = g * 2 + o2
                        if oc >= 4:
                            tensor.wait_ge(s.sem(cm1e), fB[cm1e] + oc - 3)
                        elif fb > 0:
                            if oc < 2:
                                tensor.wait_ge(s.sem(cm1e), fB[cm1e])
                            else:
                                tensor.wait_ge(s.sem(cbe), fB[cbe])
                        bank = s.PB[MBK[oc % 4]]
                        ins = None
                        for dc in range(DC):
                            ins = tensor.matmul(bank[:, 0:FB],
                                                b["ST"][g % 2][:, dc * 256 + o2 * 128:
                                                               dc * 256 + o2 * 128 + 128],
                                                XQT[:, dc * FB:(dc + 1) * FB],
                                                start=(dc == 0), stop=(dc == DC - 1))
                        s.inc(ins, cm1, 1)
                        if oc == 4:
                            tensor.wait_ge(s.sem(cse), fB[cse])
                            tensor.wait_ge(s.sem(cm2e), fB[cm2e])
                        if oc >= 4:
                            stat_mm(tensor, oc - 4)
                for j in range(OC - 4, OC):
                    stat_mm(tensor, j)

            @block.vector
            def _(vector):
                vector.wait_ge(s.sem("F_csb2"), s.val("F_csb2"))
                for oc in range(OC):
                    vector.wait_ge(s.sem(cm1), fB[cm1] + oc + 1)
                    ins = vector.tensor_mul(b["TF"][:, :], s.PB[MBK[oc % 4]][:, 0:FB],
                                            s.bcd[:, :])
                    s.inc(ins, cm1e, 1)
                    vector.drain()
                    ins = vector.tensor_scalar(b["TF"][:, :], b["TF"][:, :],
                                               s.b1r_sb[:, oc:oc + 1], None, ALU.add)
                    s.inc(ins, chz, 1)
                    vector.wait_ge(s.sem(ctg), fB[ctg] + oc + 1)
                    vector.drain()
                    ins = vector.scalar_tensor_tensor(H[:, oc * FB:(oc + 1) * FB],
                                                      b["TG"][oc % 2][:, :],
                                                      1.0, b["TF"][:, :], ALU.add, ALU.mult)
                    s.inc(ins, chh, 1)
                    vector.drain()

            @block.scalar
            def _(scalar):
                for oc in range(OC):
                    scalar.wait_ge(s.sem(chz), fB[chz] + oc + 1)
                    ins = scalar.activation(b["TG"][oc % 2][:, :], b["TF"][:, :],
                                            AF.Tanh, scale=0.5)
                    s.inc(ins, ctg, 1)
                    scalar.wait_ge(s.sem(chh), fB[chh] + oc + 1)
                    if oc >= 4:
                        scalar.wait_ge(s.sem(cstm), fB[cstm] + 2 * (oc - 3))
                    elif fb > 0:
                        scalar.wait_ge(s.sem(cstm), fB[cstm])
                    ins = scalar.activation(H[:, oc * FB:(oc + 1) * FB],
                                            H[:, oc * FB:(oc + 1) * FB],
                                            AF.Gelu, scale=0.5)
                    s.inc(ins, cgl, 1)
                    if oc >= 1:
                        ins = scalar.activation(b["HB"][(oc - 1) % 4][:, :],
                                                H[:, (oc - 1) * FB:oc * FB], AF.Identity)
                        s.inc(ins, chb, 1)
                        ins = scalar.activation(b["HS"][(oc - 1) % 4][:, :],
                                                H[:, (oc - 1) * FB:oc * FB], AF.Square)
                        s.inc(ins, chs, 1)
                scalar.drain()
                ins = scalar.activation(b["HB"][(OC - 1) % 4][:, :],
                                        H[:, (OC - 1) * FB:OC * FB], AF.Identity)
                s.inc(ins, chb, 1)
                ins = scalar.activation(b["HS"][(OC - 1) % 4][:, :],
                                        H[:, (OC - 1) * FB:OC * FB], AF.Square)
                s.inc(ins, chs, 1)

            @block.scalar
            def _(scalar):
                scalar.wait_ge(s.sem(cstm), fB[cstm] + 2 * OC)
                scalar.copy(s.stat_e[0:3, 0:FB], s.PB[2][0:3, 0:FB])
                ins = scalar.copy(s.stat_e2[0:2, 0:FB], s.PB[3][0:2, 0:FB])
                s.inc(ins, cse, 1)

            @block.tensor
            def _(tensor):
                tensor.wait_ge(s.sem(cse), s.val(cse))
                ins = None
                for t in range(NT):
                    tensor.transpose(s.PB[4][:, t * 256:t * 256 + 3],
                                     s.stat_e[0:3, t * 128:(t + 1) * 128], s.ident_f[0:3, 0:3])
                    ins = tensor.transpose(s.PB[4][:, t * 256 + 128:t * 256 + 130],
                                           s.stat_e2[0:2, t * 128:(t + 1) * 128],
                                           s.ident_f[0:2, 0:2])
                s.inc(ins, cstt, 1)

            @block.vector
            def _(vector):
                vector.wait_ge(s.sem(cstt), s.val(cstt))
                vector.wait_ge(s.sem("c_eb2"), s.val("c_eb2"))
                inv = 1.0 / INNER
                for t in range(NT):
                    sh = s.PB[4][:, t * 256:t * 256 + 1]
                    sg2h = s.PB[4][:, t * 256 + 1:t * 256 + 2]
                    sgbh = s.PB[4][:, t * 256 + 2:t * 256 + 3]
                    shs = s.PB[4][:, t * 256 + 128:t * 256 + 129]
                    sg2hs = s.PB[4][:, t * 256 + 129:t * 256 + 130]
                    vector.tensor_scalar_mul(s.fst[:, 4 + t:5 + t], sh, inv)
                    vector.drain()
                    vector.tensor_mul(s.fst[:, 0:1], s.fst[:, 4 + t:5 + t], s.fst[:, 4 + t:5 + t])
                    vector.drain()
                    vector.tensor_scalar_mul(s.fst[:, 1:2], shs, inv)
                    vector.drain()
                    vector.tensor_sub(s.fst[:, 6 + t:7 + t], s.fst[:, 1:2], s.fst[:, 0:1])
                    vector.drain()
                    vector.tensor_mul(s.fst[:, 1:2], s.fst[:, 4 + t:5 + t], sg2h)
                    vector.drain()
                    vector.tensor_scalar_mul(s.fst[:, 1:2], s.fst[:, 1:2], -2.0)
                    vector.drain()
                    vector.tensor_scalar(s.fst[:, 2:3], sg2hs, 1.0, None, ALU.mult)
                    vector.drain()
                    vector.tensor_add(s.fst[:, 1:2], s.fst[:, 1:2], s.fst[:, 2:3])
                    vector.drain()
                    vector.tensor_scalar_mul(s.fst[:, 1:2], s.fst[:, 1:2], inv)
                    vector.drain()
                    vector.tensor_mul(s.fst[:, 2:3], s.fst[:, 0:1], s.cst_sb[:, 0:1])
                    vector.drain()
                    vector.tensor_scalar_mul(s.fst[:, 2:3], s.fst[:, 2:3], inv)
                    vector.drain()
                    vector.tensor_add(s.fst[:, 8 + t:9 + t], s.fst[:, 1:2], s.fst[:, 2:3])
                    vector.drain()
                    vector.tensor_mul(s.fst[:, 1:2], s.fst[:, 4 + t:5 + t], s.cst_sb[:, 1:2])
                    vector.drain()
                    vector.tensor_sub(s.fst[:, 1:2], sgbh, s.fst[:, 1:2])
                    vector.drain()
                    vector.tensor_scalar_mul(s.fst[:, 10 + t:11 + t], s.fst[:, 1:2], inv)
                    vector.drain()
                ins = vector.nop()
                s.inc(ins, ctok, 1)

            @block.scalar
            def _(scalar):
                scalar.wait_ge(s.sem(ctok), s.val(ctok))
                ins = None
                for t in range(NT):
                    ins = scalar.activation(s.fst[:, 0 + t:1 + t], s.fst[:, 6 + t:7 + t],
                                            AF.Sqrt, bias=s.eps_col[:, 0:1])
                s.inc(ins, crr, 1)

            @block.vector
            def _(vector):
                vector.wait_ge(s.sem(crr), s.val(crr))
                for t in range(NT):
                    vector.reciprocal(s.fst[:, 24 + t:25 + t], s.fst[:, 0 + t:1 + t])
                vector.drain()
                ins = None
                for t in range(NT):
                    vector.tensor_mul(s.fst[:, 0:1], s.fst[:, 24 + t:25 + t],
                                      s.fst[:, 24 + t:25 + t])
                    vector.drain()
                    vector.tensor_mul(s.fst[:, 0:1], s.fst[:, 0:1], s.fst[:, 8 + t:9 + t])
                    vector.drain()
                    vector.tensor_mul(s.fst[:, 1:2], s.fst[:, 24 + t:25 + t],
                                      s.fst[:, 10 + t:11 + t])
                    vector.drain()
                    vector.tensor_scalar_mul(s.fst[:, 1:2], s.fst[:, 1:2], 2.0)
                    vector.drain()
                    vector.tensor_add(s.fst[:, 0:1], s.fst[:, 0:1], s.fst[:, 1:2])
                    vector.drain()
                    ins = vector.tensor_add(s.fst[:, 12 + t:13 + t], s.fst[:, 0:1],
                                            s.cst_sb[:, 3:4])
                    vector.drain()
                s.inc(ins, cpk, 1)

            @block.scalar
            def _(scalar):
                scalar.wait_ge(s.sem(cpk), s.val(cpk))
                ins = None
                for t in range(NT):
                    ins = scalar.activation(s.fst[:, 0 + t:1 + t], s.fst[:, 12 + t:13 + t],
                                            AF.Sqrt, bias=s.eps_col[:, 0:1])
                s.inc(ins, crr2, 1)

            @block.vector
            def _(vector):
                vector.wait_ge(s.sem(crr2), s.val(crr2))
                ins = None
                for t in range(NT):
                    ins = vector.reciprocal(s.fst[:, 26 + t:27 + t], s.fst[:, 0 + t:1 + t])
                vector.drain()
                s.inc(ins, cpk2, 1)

            @block.tensor
            def _(tensor):
                tensor.wait_ge(s.sem(cpk2), s.val(cpk2))
                ins = None
                for t in range(NT):
                    tensor.transpose(s.PB[5][0:1, t * 128:(t + 1) * 128],
                                     s.fst[:, 4 + t:5 + t], s.ident_f[:, :])
                    ins = tensor.transpose(s.PB[5][0:1, FB + t * 128:FB + (t + 1) * 128],
                                           s.fst[:, 24 + t:25 + t], s.ident_f[:, :])
                s.inc(ins, ctr2, 1)

            @block.scalar
            def _(scalar):
                scalar.wait_ge(s.sem(ctr2), s.val(ctr2))
                ins = scalar.copy(s.srow[0:1, 0:2 * FB], s.PB[5][0:1, 0:2 * FB])
                s.inc(ins, cte3, 1)

            f1 = self.bcast_row(s.srow[0:1, 0:FB], s.bca[:, :], FB, 6, "F_ba",
                                cte3, s.val(cte3))
            f2 = self.bcast_row(s.srow[0:1, FB:2 * FB], s.bcb[:, :], FB, 7, "F_bb",
                                f1, s.val(f1))

            if pb >= 0:
                emit_b_w2(pb, fB)

            @block.vector
            def _(vector):
                vector.wait_ge(s.sem(f2), s.val(f2))
                for oc in range(OC):
                    vector.tensor_sub(b["TF"][:, :], H[:, oc * FB:(oc + 1) * FB], s.bca[:, :])
                    vector.drain()
                    vector.tensor_mul(b["TF"][:, :], b["TF"][:, :], s.bcb[:, :])
                    vector.drain()
                    ins = vector.tensor_scalar(H[:, oc * FB:(oc + 1) * FB], b["TF"][:, :],
                                               s.gr_sb[:, oc:oc + 1], s.brr_sb[:, oc:oc + 1],
                                               ALU.mult, ALU.add)
                    s.inc(ins, "F_hp", 1)
                    vector.drain()
                    if oc == 0:
                        vector.tensor_copy(b["RM"][:, :], H[:, 0:FB])
                        vector.tensor_copy(b["RN"][:, :], H[:, 0:FB])
                    else:
                        vector.tensor_max(b["RM"][:, :], b["RM"][:, :],
                                          H[:, oc * FB:(oc + 1) * FB])
                        vector.tensor_tensor(b["RN"][:, :], b["RN"][:, :],
                                             H[:, oc * FB:(oc + 1) * FB], ALU.min)
                    vector.drain()
                    if pb >= 0 and oc % 4 == 3:
                        evac_mm2(vector, pb, fB, oc // 4)
                        vector.drain()
                vector.tensor_scalar_mul(b["RN"][:, :], b["RN"][:, :], -1.0)
                vector.drain()
                ins = vector.tensor_max(b["RM"][:, :], b["RM"][:, :], b["RN"][:, :])
                vector.drain()
                s.inc(ins, camx, 1)

            if pb >= 0:
                emit_b_mm2(pb, fB, fB[cse] + 1)
                emit_b_ffn(pb, fB)

            @block.tensor
            def _(tensor):
                tensor.wait_ge(s.sem(camx), s.val(camx))
                ins = None
                for t in range(NT):
                    ins = tensor.transpose(s.PB[4][:, t * 128:(t + 1) * 128],
                                           b["RM"][:, t * 128:(t + 1) * 128], s.ident_f[:, :])
                s.inc(ins, ctm, 1)

            @block.vector
            def _(vector):
                vector.wait_ge(s.sem(ctm), s.val(ctm))
                ins = None
                for t in range(NT):
                    vector.reduce_max(s.fst[:, 0:1], s.PB[4][:, t * 128:(t + 1) * 128], AX.X)
                    vector.drain()
                    vector.tensor_mul(s.fst[:, 0:1], s.fst[:, 0:1], s.fst[:, 26 + t:27 + t])
                    vector.drain()
                    vector.tensor_scalar_max(s.fst[:, 0:1], s.fst[:, 0:1], 1e-5)
                    vector.drain()
                    vector.reciprocal(s.fst[:, 1:2], s.fst[:, 0:1])
                    vector.drain()
                    vector.tensor_mul(s.fst[:, 28 + t:29 + t], s.fst[:, 26 + t:27 + t],
                                      s.fst[:, 1:2])
                    vector.drain()
                    vector.tensor_scalar_mul(s.fst[:, 28 + t:29 + t], s.fst[:, 28 + t:29 + t],
                                             127.0)
                    vector.drain()
                    vector.tensor_mul(s.fst[:, 30 + t:31 + t], s.fst[:, 0:1], s.m2h_sb[:, 0:1])
                    vector.drain()
                    ins = vector.tensor_scalar_mul(s.fst[:, 30 + t:31 + t],
                                                   s.fst[:, 30 + t:31 + t], 1.0 / 127.0)
                    vector.drain()
                s.inc(ins, ctk2, 1)

            @block.tensor
            def _(tensor):
                tensor.wait_ge(s.sem(ctk2), s.val(ctk2))
                ins = None
                for t in range(NT):
                    tensor.transpose(s.PB[5][0:1, t * 128:(t + 1) * 128],
                                     s.fst[:, 28 + t:29 + t], s.ident_f[:, :])
                    ins = tensor.transpose(s.PB[5][0:1, FB + t * 128:FB + (t + 1) * 128],
                                           s.fst[:, 30 + t:31 + t], s.ident_f[:, :])
                s.inc(ins, ctr3, 1)

            @block.scalar
            def _(scalar):
                scalar.wait_ge(s.sem(ctr3), s.val(ctr3))
                ins = scalar.copy(s.srow[0:1, 0:2 * FB], s.PB[5][0:1, 0:2 * FB])
                s.inc(ins, cte5, 1)

            f3 = self.bcast_row(s.srow[0:1, 0:FB], s.bcc[:, :], FB, 6, "F_bc",
                                cte5, s.val(cte5))
            f4 = self.bcast_row(s.srow[0:1, FB:2 * FB], s.bce[:, :], FB, 7, "F_bd",
                                f3, s.val(f3))

            if fb + 1 < c.NFB:
                emit_a1(fb + 1)

            @block.vector
            def _(vector):
                vector.wait_ge(s.sem(f3), s.val(f3))
                if pb >= 0:
                    vector.wait_ge(s.sem(cm2), fB[cm2] + 2 * DC)
                for oc in range(OC):
                    vector.tensor_mul(b["TF"][:, :], H[:, oc * FB:(oc + 1) * FB], s.bcc[:, :])
                    vector.drain()
                    ins = vector.tensor_scalar(Q2[:, oc * FB:(oc + 1) * FB], b["TF"][:, :],
                                               MAGIC, MAGIC, ALU.add, ALU.subtract)
                    s.inc(ins, cq2, 1)
                    vector.drain()

            if pb >= 0:
                emit_b_base(pb, fB)

        pb = c.NFB - 1
        fT = {nm: s.val(nm) for nm in REL}
        emit_b_w2(pb, fT)
        emit_b_mm2(pb, fT, fT[cse])

        @block.vector
        def _(vector):
            for dc in range(DC):
                evac_mm2(vector, pb, fT, dc)
                vector.drain()

        emit_b_ffn(pb, fT)
        emit_b_base(pb, fT)

        self.barrier()
        self.release_prefix("F_")

    def phase_att(self, b):
        s, c, block = self, self.c, self.block
        D, DC, KV = c.D, c.DC, c.KV
        NKB = s.NKB
        HDC = DC // 2

        @block.gpsimd
        def _(gpsimd):
            for dci in range(DC):
                s.inc(gpsimd.dma_start(out=b["WO"][:, dci * D:(dci + 1) * D],
                                       in_=s.wo[dci * 128:(dci + 1) * 128, :]), "Tdwo", 16)
            gpsimd.wait_ge(s.sem("Tdwo"), s.val("Tdwo"))

        for nm in ("Tmm", "Tkc", "Tbm", "Tsc", "Tmx", "Tcc", "Trc", "Tdt", "Tptm",
                   "Tpte", "Tptd", "Tavm", "Tavc", "Tave", "Twom", "Twoe", "Ttrm",
                   "Tyv", "Tyo0", "Tyo1", "Tex0", "Tex1", "Tex2", "Tex3",
                   "Tdq0", "Tdq1", "Tkt0", "Tkt1", "Tvb0", "Tvb1", "Tbp0", "Tbp1"):
            s.sem(nm)

        nktl = 0
        nvl = 0
        nbl = 0
        s.ktfree = []
        s.vbfree = []

        for g in range(min(c.NG, getattr(self, "ng_limit", 99))):
            ctx_kb = 2 * (g + 1)
            ctx_kc = 8 * (g + 1)
            q0 = g * 512

            def kbmax(il, g=g):
                return (4 * g + il + 2) // 2

            def kcmax(il, g=g):
                return 2 * (4 * g + il + 1)

            @block.sync
            def _(sync):
                s.wsnap(sync, "Twom")
                src = s.qt_scr[:, q0:q0 + 512].rearrange("(a p) cc -> p a cc", p=128)
                dst = b["QA"].rearrange("p (a cc) -> p a cc", cc=512)
                s.inc(sync.dma_start(out=dst, in_=src), f"Tdq{g % 2}", 16)

            tmm_run = s.val("Tmm")
            for kb in range(ctx_kb):
                tmm_run += len([il for il in range(4) if kb < kbmax(il)])
                s.ktfree.append(tmm_run)
                s.ktfree.append(tmm_run)

            @block.sync
            def _(sync):
                nonlocal nktl
                for kb in range(ctx_kb):
                    for hh in range(2):
                        if nktl >= 2:
                            sync.wait_ge(s.sem("Tmm"), s.ktfree[nktl - 2])
                        src = s.kt_scr[hh * HDC * 128:(hh + 1) * HDC * 128,
                                       kb * 512:(kb + 1) * 512].rearrange(
                            "(a p) cc -> p a cc", p=128)
                        dst = b["KB"][nktl % 2].rearrange("p (a cc) -> p a cc", cc=512)
                        s.inc(sync.dma_start(out=dst, in_=src), f"Tkt{nktl % 2}", 16)
                        nktl += 1

            exp_seen = [s.val(f"Tex{il}") for il in range(4)]

            @block.tensor
            def _(tensor):
                tensor.wait_ge(s.sem(f"Tdq{g % 2}"), 16 * (g // 2 + 1))
                nktc = s.cnt.get("_Tktc", 0)
                nexp = list(exp_seen)
                for kb in range(ctx_kb):
                    parts = [il for il in range(4) if kb < kbmax(il)]
                    for hh in range(2):
                        tensor.wait_ge(s.sem(f"Tkt{nktc % 2}"), 16 * (nktc // 2 + 1))
                        for il in parts:
                            bank = s.PB[il]
                            if hh == 0 and kb > 0:
                                tensor.wait_ge(s.sem(f"Tex{il}"), nexp[il])
                            for d8 in range(HDC):
                                dc = hh * HDC + d8
                                last = tensor.matmul(
                                    bank[:, :],
                                    b["QA"][:, dc * 512 + il * 128: dc * 512 + il * 128 + 128],
                                    b["KB"][nktc % 2][:, d8 * 512:(d8 + 1) * 512],
                                    start=(dc == 0), stop=(dc == DC - 1))
                            if hh == 1:
                                s.inc(last, "Tmm", 1)
                                nexp[il] += 1
                        nktc += 1
                s.cnt["_Tktc"] = nktc

            @block.vector
            def _(vector):
                nmm = s.cnt.get("_Tmm", 0)
                for kb in range(ctx_kb):
                    parts = [il for il in range(4) if kb < kbmax(il)]
                    for il in parts:
                        nmm += 1
                        vector.wait_ge(s.sem("Tmm"), nmm)
                        i = 4 * g + il
                        bank = s.PB[il]
                        if kb == kbmax(il) - 1:
                            moff = 256 * i - 512 * kb
                            vector.tensor_add(bank[:, moff:moff + 128],
                                              bank[:, moff:moff + 128], s.mask_sb[:, 0:128])
                            vector.tensor_add(bank[:, moff + 128:moff + 256],
                                              bank[:, moff + 128:moff + 256],
                                              s.mask_sb[:, 128:256])
                            if moff == 0:
                                vector.tensor_add(bank[:, 256:384], bank[:, 256:384],
                                                  s.mask_sb[:, 256:384])
                                vector.tensor_add(bank[:, 384:512], bank[:, 384:512],
                                                  s.mask_sb[:, 256:384])
                            vector.drain()
                        vector.reduce_max(s.bm_sb[:, il * NKB + kb:il * NKB + kb + 1],
                                          bank[:, :], AX.X)
                        vector.drain()
                        ins = vector.tensor_scalar_mul(
                            s.fst[:, il:il + 1],
                            s.bm_sb[:, il * NKB + kb:il * NKB + kb + 1], -c.SCALE)
                        s.inc(ins, "Tbm", 1)
                        vector.drain()
                s.cnt["_Tmm"] = nmm

            @block.scalar
            def _(scalar):
                s.wsnap(scalar, "Tptm") if g > 0 else None
                nbm = s.cnt.get("_Tbm", 0)
                for kb in range(ctx_kb):
                    parts = [il for il in range(4) if kb < kbmax(il)]
                    for il in parts:
                        nbm += 1
                        scalar.wait_ge(s.sem("Tbm"), nbm)
                        ins = scalar.activation(
                            b["P"][:, il * KV + kb * 512: il * KV + kb * 512 + 512],
                            s.PB[il][:, :], AF.Exp, scale=c.SCALE, bias=s.fst[:, il:il + 1],
                            accum_out=s.lb_sb[:, il * NKB + kb:il * NKB + kb + 1])
                        s.inc(ins, f"Tex{il}", 1)
                s.cnt["_Tbm"] = nbm

            @block.vector
            def _(vector):
                for il in range(4):
                    vector.wait_ge(s.sem(f"Tex{il}"), s.val(f"Tex{il}"))
                ins = None
                for il in range(4):
                    nkb = kbmax(il)
                    vector.reduce_max(s.fst[:, 8:9], s.bm_sb[:, il * NKB: il * NKB + nkb], AX.X)
                    vector.drain()
                    ins = vector.tensor_scalar_mul(s.fst[:, il:il + 1], s.fst[:, 8:9], -c.SCALE)
                    vector.drain()
                s.inc(ins, "Tmx", 1)

            @block.scalar
            def _(scalar):
                s.wsnap(scalar, "Tmx")
                ins = None
                for il in range(4):
                    nkb = kbmax(il)
                    ins = scalar.activation(s.cc_sb[:, il * NKB: il * NKB + nkb],
                                            s.bm_sb[:, il * NKB: il * NKB + nkb],
                                            AF.Exp, scale=c.SCALE, bias=s.fst[:, il:il + 1])
                s.inc(ins, "Tcc", 1)

            @block.vector
            def _(vector):
                s.wsnap(vector, "Tcc")
                s.wsnap(vector, "Tavm")
                for il in range(4):
                    nkb = kbmax(il)
                    vector.tensor_mul(s.rc_sb[:, il * NKB: il * NKB + nkb],
                                      s.lb_sb[:, il * NKB: il * NKB + nkb],
                                      s.cc_sb[:, il * NKB: il * NKB + nkb])
                    vector.drain()
                    vector.reduce_sum(s.ll_sb[:, il:il + 1],
                                      s.rc_sb[:, il * NKB: il * NKB + nkb], AX.X)
                    vector.drain()
                    vector.reciprocal(s.ll_sb[:, il:il + 1], s.ll_sb[:, il:il + 1])
                    vector.drain()
                    vector.tensor_scalar(s.rc_sb[:, il * NKB: il * NKB + nkb],
                                         s.cc_sb[:, il * NKB: il * NKB + nkb],
                                         s.ll_sb[:, il:il + 1], None, ALU.mult)
                    vector.drain()
                ins = None
                for il in range(4):
                    for kc in range(kcmax(il), ctx_kc):
                        ins = vector.memset(
                            b["PT"][:, kc * 512 + il * 128: kc * 512 + il * 128 + 128], 0.0)
                if ins is None:
                    ins = vector.tensor_copy(s.ll_sb[:, 0:1], s.ll_sb[:, 0:1])
                s.inc(ins, "Trc", 1)

            if self.astage < 2:
                self.barrier()
                continue

            @block.vector
            def _(vector):
                s.wsnap(vector, "Trc")
                npt = s.cnt.get("_Tdt", 0)
                for il in range(4):
                    for kc in range(kcmax(il)):
                        if npt >= 1:
                            vector.wait_ge(s.sem("Tptm"), npt)
                        kb = kc // 4
                        ins = vector.tensor_scalar(
                            s.dtile[:, :], s.ident_f[:, :],
                            s.rc_sb[:, il * NKB + kb: il * NKB + kb + 1], None, ALU.mult)
                        s.inc(ins, "Tdt", 1)
                        npt += 1
                s.cnt["_Tdt"] = npt

            @block.tensor
            def _(tensor):
                npt = s.cnt.get("_Tptm", 0)
                for il in range(4):
                    for kc in range(kcmax(il)):
                        npt += 1
                        tensor.wait_ge(s.sem("Tdt"), npt)
                        if npt >= 3:
                            tensor.wait_ge(s.sem("Tpte"), npt - 2)
                        bank = s.PB[4 + (npt % 2)]
                        ins = tensor.matmul(
                            bank[:, 0:128],
                            b["P"][:, il * KV + kc * 128: il * KV + kc * 128 + 128],
                            s.dtile[:, :], start=True, stop=True)
                        s.inc(ins, "Tptm", 1)
                s.cnt["_Tptm"] = npt

            @block.scalar
            def _(scalar):
                npt = s.cnt.get("_Tpte", 0)
                for il in range(4):
                    for kc in range(kcmax(il)):
                        npt += 1
                        scalar.wait_ge(s.sem("Tptm"), npt)
                        bank = s.PB[4 + (npt % 2)]
                        ins = scalar.copy(
                            b["PT"][:, kc * 512 + il * 128: kc * 512 + il * 128 + 128],
                            bank[:, 0:128])
                        s.inc(ins, "Tpte", 1)
                s.cnt["_Tpte"] = npt

            if self.astage < 3:
                self.barrier()
                continue

            NPASS = (DC + 7) // 8

            def pass_dcs(ps):
                return list(range(ps * 8, min(DC, ps * 8 + 8)))

            avc_run, avm_run = s.val("Tavc"), s.val("Tavm")
            for ps in range(NPASS):
                nb = len(pass_dcs(ps))
                for kc in range(ctx_kc):
                    if kc == ctx_kc - 1:
                        avm_run += nb
                        s.vbfree.append(("Tavm", avm_run))
                    else:
                        avc_run += 1
                        s.vbfree.append(("Tavc", avc_run))

            @block.sync
            def _(sync):
                nonlocal nvl
                for ps in range(NPASS):
                    for kc in range(ctx_kc):
                        if nvl >= 2:
                            nm, v = s.vbfree[nvl - 2]
                            sync.wait_ge(s.sem(nm), v)
                        s.inc(sync.dma_start(out=b["VB"][nvl % 2][:, :],
                                             in_=s.v_scr[kc * 128:(kc + 1) * 128, :]),
                              f"Tvb{nvl % 2}", 16)
                        nvl += 1

            @block.tensor
            def _(tensor):
                s.wsnap(tensor, "Tpte")
                s.wsnap(tensor, "Trc")
                s.wsnap(tensor, "Twoe")
                nvc = s.cnt.get("_Tvc", 0)
                ave_base = s.val("Tave")
                done = 0
                for ps in range(NPASS):
                    dcs = pass_dcs(ps)
                    for kc in range(ctx_kc):
                        tensor.wait_ge(s.sem(f"Tvb{nvc % 2}"), 16 * (nvc // 2 + 1))
                        if ps >= 1 and kc == 0:
                            tensor.wait_ge(s.sem("Tave"), ave_base + done)
                        for d8, dc in enumerate(dcs):
                            last = tensor.matmul(
                                s.PB[d8][:, :],
                                b["VB"][nvc % 2][:, dc * 128:(dc + 1) * 128],
                                b["PT"][:, kc * 512:(kc + 1) * 512],
                                start=(kc == 0), stop=(kc == ctx_kc - 1))
                        if kc == ctx_kc - 1:
                            s.inc(last, "Tavm", len(dcs))
                        else:
                            s.inc(last, "Tavc", 1)
                        nvc += 1
                    done += len(dcs)
                s.cnt["_Tvc"] = nvc

            @block.scalar
            def _(scalar):
                avm_base = s.cnt.get("_Tavm", 0)
                done = 0
                for ps in range(NPASS):
                    dcs = pass_dcs(ps)
                    done += len(dcs)
                    scalar.wait_ge(s.sem("Tavm"), avm_base + done)
                    for d8, dc in enumerate(dcs):
                        ins = scalar.copy(b["QA"][:, dc * 512:(dc + 1) * 512], s.PB[d8][:, :])
                        s.inc(ins, "Tave", 1)
                s.cnt["_Tavm"] = avm_base + done

            if self.astage < 4:
                self.barrier()
                continue

            NP = 4 * DC
            nyv_base = s.cnt.get("_Tyv", 0)
            tyo_base = [s.val("Tyo0"), s.val("Tyo1")]
            nbl0 = nbl

            @block.tensor
            def _(tensor):
                s.wsnap(tensor, "Tave")
                s.wsnap(tensor, "Tdwo")
                nwo = s.cnt.get("_Twom", 0)
                twoe_base = s.val("Twoe") - 0
                ttrm_base = s.cnt.get("_Ttrm", 0)

                def ytr4(dco):
                    for t in range(4):
                        j = 4 * dco + t
                        if t == 0:
                            tensor.wait_ge(s.sem("Twoe"), twoe_base + dco + 1)
                        if j >= 2 and self.astage >= 6:
                            tensor.wait_ge(s.sem("Tyv"), nyv_base + j - 1)
                        bank = s.PB[6 + (j % 2)]
                        ins = tensor.transpose(bank[:, 0:128],
                                               b["AT"][:, t * 128:(t + 1) * 128],
                                               s.ident_f[:, :])
                        s.inc(ins, "Ttrm", 1)

                for dco in range(DC):
                    bank = s.PB[dco % 2]
                    if dco >= 2:
                        tensor.wait_ge(s.sem("Twoe"), twoe_base + dco - 1)
                    ins = None
                    for dci in range(DC):
                        ins = tensor.matmul(
                            bank[:, :],
                            b["WO"][:, dci * D + dco * 128: dci * D + dco * 128 + 128],
                            b["QA"][:, dci * 512:(dci + 1) * 512],
                            start=(dci == 0), stop=(dci == DC - 1))
                    s.inc(ins, "Twom", 1)
                    if self.astage >= 5 and dco >= 1:
                        ytr4(dco - 1)
                if self.astage >= 5:
                    ytr4(DC - 1)
                s.cnt["_Twom"] = nwo + DC
                s.cnt["_Ttrm"] = ttrm_base + (NP if self.astage >= 5 else 0)

            @block.scalar
            def _(scalar):
                nwo = s.cnt.get("_Twoe", 0)
                ttrm_base = s.cnt.get("_Ttrm", 0) - (NP if self.astage >= 5 else 0)
                for dco in range(DC):
                    scalar.wait_ge(s.sem("Twom"), nwo + dco + 1)
                    if dco >= 1 and self.astage >= 5:
                        scalar.wait_ge(s.sem("Ttrm"), ttrm_base + 4 * dco)
                    ins = scalar.copy(b["AT"][:, :], s.PB[dco % 2][:, :])
                    s.inc(ins, "Twoe", 1)
                s.cnt["_Twoe"] = nwo + DC


            if self.astage < 7:
                self.barrier()
                continue

            @block.sync
            def _(sync):
                nonlocal nbl
                for j in range(NP):
                    dco, t = j // 4, j % 4
                    if j >= 2:
                        sync.wait_ge(s.sem("Tyv"), nyv_base + j - 1)
                    s.inc(sync.dma_start(
                        out=b["BP"][(nbl0 + j) % 2][:, :],
                        in_=s.base_scr[q0 + t * 128: q0 + (t + 1) * 128,
                                       dco * 128:(dco + 1) * 128]),
                        f"Tbp{(nbl0 + j) % 2}", 16)
                    if j >= 2 and self.astage >= 9:
                        m = j - 2
                        s.inc(sync.dma_start(
                            out=s.y[q0 + (m % 4) * 128: q0 + (m % 4 + 1) * 128,
                                    (m // 4) * 128:(m // 4 + 1) * 128],
                            in_=b["OB"][m % 2][:, :]), f"Tyo{m % 2}", 16)
                if self.astage >= 9:
                    for m in (NP - 2, NP - 1):
                        sync.wait_ge(s.sem("Tyv"), nyv_base + m + 1)
                        s.inc(sync.dma_start(
                            out=s.y[q0 + (m % 4) * 128: q0 + (m % 4 + 1) * 128,
                                    (m // 4) * 128:(m // 4 + 1) * 128],
                            in_=b["OB"][m % 2][:, :]), f"Tyo{m % 2}", 16)
                nbl = nbl0 + NP

            if self.astage < 5:
                self.barrier()
                continue

            if self.astage < 6:
                self.barrier()
                continue

            @block.vector
            def _(vector):
                ttrm_base = s.cnt.get("_Ttrm", 0) - NP
                for j in range(NP):
                    vector.wait_ge(s.sem("Ttrm"), ttrm_base + j + 1)
                    pidx = nbl0 + j
                    if self.astage >= 7:
                        vector.wait_ge(s.sem(f"Tbp{pidx % 2}"), 16 * (pidx // 2 + 1))
                    if j >= 2 and self.astage >= 9:
                        pj = j - 2
                        vector.wait_ge(s.sem(f"Tyo{pj % 2}"),
                                       tyo_base[pj % 2] + 16 * (pj // 2 + 1))
                    elif self.astage >= 9:
                        vector.wait_ge(s.sem(f"Tyo{j % 2}"), tyo_base[j % 2])
                    bank = s.PB[6 + (j % 2)]
                    ins = vector.tensor_add(b["OB"][j % 2][:, :], bank[:, 0:128],
                                            b["BP"][pidx % 2][:, :])
                    s.inc(ins, "Tyv", 1)
                    vector.drain()
                s.cnt["_Tyv"] = nyv_base + NP

            self.barrier()

        @block.sync
        def _(sync):
            sync.wait_ge(s.sem("Tyo0"), s.val("Tyo0"))
            sync.wait_ge(s.sem("Tyo1"), s.val("Tyo1"))

    def build(self):
        self.declare_dram()
        self.declare_persistent()
        c, nc = self.c, self.nc
        D, KV, DC, OC, FB = c.D, c.KV, c.DC, c.OC, c.FB

        def alloc(st, specs):
            out = {}
            for name, shape, dt in specs:
                out[name] = st.enter_context(nc.sbuf_tensor(name, shape, dt)).ap()
            return out

        with self.ctx:
            for i in range(8):
                self.sem(f"bar{i}")
            with nc.Block() as block:
                self.block = block
                self.phase_init()
                self.barrier()
                if self.upto >= 1:
                    st = ExitStack()
                    a = alloc(st, [("qW0", [128, 2048], F32), ("qW1", [128, 2048], F32),
                                   ("qS0", [128, 2048], BF16), ("qS1", [128, 2048], BF16),
                                   ("qT0", [128, 2048], BF16), ("qT1", [128, 2048], BF16)])
                    self.phase_q([a["qW0"], a["qW1"]], [a["qS0"], a["qS1"]],
                                 [a["qT0"], a["qT1"]])
                    st.close()
                    self.barrier()
                    self.release_prefix("dq_", "cq_", "rq", "sq_", "mq", "QM", "CST",
                                        "c_init", "d_init", "bar")
                if self.upto >= 2:
                    stx = ExitStack()
                    self.xnT = stx.enter_context(
                        nc.sbuf_tensor("xnT", [128, DC * KV], BF16)).ap()
                    st = ExitStack()
                    a = alloc(st, [("aXA", [128, D], F32), ("aXN0", [128, D], F32),
                                   ("aXN1", [128, D], F32), ("aSQ", [128, D], BF16)])
                    self.phase_a(dict(XA=a["aXA"], XNF=[a["aXN0"], a["aXN1"]],
                                      SQD=a["aSQ"]))
                    st.close()
                    self.barrier()
                    self.release_prefix("A", "bar")
                    if self.upto >= 3:
                        st = ExitStack()
                        a = alloc(st, [("kKH", [128, DC * 512], F32),
                                       ("kWC0", [128, D], BF16), ("kWC1", [128, D], BF16),
                                       ("kTR0", [128, 1024], F32), ("kTR1", [128, 1024], F32),
                                       ("kKO0", [128, 512], BF16), ("kKO1", [128, 512], BF16),
                                       ("kRT0", [128, 512], F32), ("kRT1", [128, 512], F32)])
                        bufs = dict(WCB=[a["kWC0"], a["kWC1"]], TRIG=[a["kTR0"], a["kTR1"]],
                                    KTOUT=[a["kKO0"], a["kKO1"]], RT=[a["kRT0"], a["kRT1"]])
                        self.phase_b1(bufs, a["kKH"])
                        st.close()
                        self.barrier()
                        self.release_prefix("K", "bar")
                    if self.upto >= 4:
                        st = ExitStack()
                        a = alloc(st, [("vWV", [128, DC * (D // 2)], BF16),
                                       ("vKO0", [128, 512], BF16), ("vKO1", [128, 512], BF16)])
                        self.phase_b2(dict(KTOUT=[a["vKO0"], a["vKO1"]]), a["vWV"])
                        st.close()
                        self.barrier()
                        self.release_prefix("V", "bar")
                    stx.close()
                if self.upto >= 5:
                    st = ExitStack()
                    a = alloc(st, [("q3XQ", [128, DC * 512], BF16),
                                   ("q3KH", [128, DC * 512], F32),
                                   ("q3XA", [128, D], F32), ("q3XN0", [128, D], F32),
                                   ("q3XN1", [128, D], F32), ("q3SQ", [128, D], BF16),
                                   ("q3WC0", [128, D], BF16), ("q3WC1", [128, D], BF16),
                                   ("q3TR0", [128, 1024], F32), ("q3TR1", [128, 1024], F32),
                                   ("q3KO0", [128, 512], BF16), ("q3KO1", [128, 512], BF16),
                                   ("q3RT0", [128, 512], F32), ("q3RT1", [128, 512], F32)])
                    bufs = dict(XA=a["q3XA"], XNF=[a["q3XN0"], a["q3XN1"]], SQD=a["q3SQ"],
                                WCB=[a["q3WC0"], a["q3WC1"]], TRIG=[a["q3TR0"], a["q3TR1"]],
                                KTOUT=[a["q3KO0"], a["q3KO1"]], RT=[a["q3RT0"], a["q3RT1"]])
                    self.phase_b3(bufs, a["q3KH"], a["q3XQ"])
                    st.close()
                    self.barrier()
                    self.release_prefix("Q", "bar")
                if self.upto >= 6:
                    st = ExitStack()
                    a = alloc(st, [("fH", [128, OC * FB], F32), ("fQ2", [128, OC * FB], BF16),
                                   ("fXA", [128, D], F32), ("fXN0", [128, D], F32),
                                   ("fXN1", [128, D], F32),
                                   ("fXQT", [128, DC * FB], BF16),
                                   ("fST0", [128, DC * 256], BF16),
                                   ("fST1", [128, DC * 256], BF16),
                                   ("fHB0", [128, FB], BF16), ("fHB1", [128, FB], BF16),
                                   ("fHB2", [128, FB], BF16), ("fHB3", [128, FB], BF16),
                                   ("fHS0", [128, FB], BF16), ("fHS1", [128, FB], BF16),
                                   ("fHS2", [128, FB], BF16), ("fHS3", [128, FB], BF16),
                                   ("fTG0", [128, FB], F32), ("fTG1", [128, FB], F32),
                                   ("fTF", [128, FB], F32),
                                   ("fTF20", [128, FB], F32), ("fTF21", [128, FB], F32),
                                   ("fRM", [128, FB], F32),
                                   ("fRN", [128, FB], F32),
                                   ("fW20", [128, (OC // 2) * 128], BF16),
                                   ("fW21", [128, (OC // 2) * 128], BF16),
                                   ("fFT", [128, DC * FB], F32),
                                   ("fBB", [128, D // 2], F32)])
                    fbufs = dict(XA=a["fXA"], XNF=[a["fXN0"], a["fXN1"]],
                                 SQD=a["fXQT"][:, 0:D],
                                 Q1=[a["fXN0"], a["fXN1"]], XQT=a["fXQT"],
                                 ST=[a["fST0"], a["fST1"]],
                                 HB=[a["fHB0"], a["fHB1"], a["fHB2"], a["fHB3"]],
                                 HS=[a["fHS0"], a["fHS1"], a["fHS2"], a["fHS3"]],
                                 TG=[a["fTG0"], a["fTG1"]],
                                 TF=a["fTF"], TF2=[a["fTF20"], a["fTF21"]],
                                 XNF0=a["fXN0"], XNF1=a["fXN1"],
                                 RM=a["fRM"], RN=a["fRN"],
                                 W2=[a["fW20"], a["fW21"]], FT=a["fFT"],
                                 BB=a["fBB"], H=a["fH"], Q2=a["fQ2"])
                    self.phase_f(fbufs)
                    st.close()
                if self.upto >= 7:
                    st = ExitStack()
                    a = alloc(st, [("tWO", [128, DC * D], BF16), ("tP", [128, 4 * KV], BF16),
                                   ("tPT", [128, 4 * KV], BF16),
                                   ("tQA", [128, DC * 512], BF16),
                                   ("tKB0", [128, (DC // 2) * 512], BF16),
                                   ("tKB1", [128, (DC // 2) * 512], BF16),
                                   ("tVB0", [128, D], BF16), ("tVB1", [128, D], BF16),
                                   ("tAT", [128, 512], F32),
                                   ("tOB0", [128, 128], F32), ("tOB1", [128, 128], F32),
                                   ("tBP0", [128, 128], F32), ("tBP1", [128, 128], F32)])
                    abufs = dict(WO=a["tWO"], P=a["tP"], PT=a["tPT"], QA=a["tQA"],
                                 KB=[a["tKB0"], a["tKB1"]], VB=[a["tVB0"], a["tVB1"]],
                                 AT=a["tAT"], OB=[a["tOB0"], a["tOB1"]],
                                 BP=[a["tBP0"], a["tBP1"]])
                    self.phase_att(abufs)
                    st.close()
        return self.nc



def host_prepare(cfg, inputs):
    c = cfg
    D, KV = c.D, c.KV
    x = np.asarray(inputs["L0_attn_q"], dtype=np.float32)
    Wq = np.asarray(inputs["L0_attn_W_q"], dtype=np.float32)
    Wk = np.asarray(inputs["L0_attn_W_k"], dtype=np.float32)
    Wv = np.asarray(inputs["L0_attn_W_v"], dtype=np.float32)
    Wo = np.asarray(inputs["L0_attn_W_o"], dtype=np.float32)
    w1 = np.asarray(inputs["L0_ffn_w1"], dtype=np.float32)
    b1 = np.asarray(inputs["L0_ffn_b1"], dtype=np.float32)
    ln_g = np.asarray(inputs["L0_ffn_ln_g"], dtype=np.float32)
    ln_b = np.asarray(inputs["L0_ffn_ln_b"], dtype=np.float32)
    w2 = np.asarray(inputs["L0_ffn_w2"], dtype=np.float32)
    b2 = np.asarray(inputs["L0_ffn_b2"], dtype=np.float32)

    perm = np.concatenate([np.arange(0, D, 2), np.arange(1, D, 2)])
    wq_p = np.ascontiguousarray(Wq[:, perm])
    wk_p = np.ascontiguousarray(Wk[:, perm])
    w1t = np.ascontiguousarray(w1.T)
    w2t = np.ascontiguousarray(w2.T)

    inv = 1.0 / (10000.0 ** (np.arange(0, D, 2, dtype=np.float64) / D))
    ang = np.outer(np.arange(KV, dtype=np.float64), inv)
    cosk = np.ascontiguousarray(np.cos(ang).T.astype(np.float32))
    sink = np.ascontiguousarray(np.sin(ang).T.astype(np.float32))

    b1r = np.ascontiguousarray(b1.reshape(c.OC, 128).T)
    gr = np.ascontiguousarray(ln_g.reshape(c.OC, 128).T)
    brr = np.ascontiguousarray(ln_b.reshape(c.OC, 128).T)
    b2r = np.ascontiguousarray(b2.reshape(c.DC, 128).T)
    ident = np.eye(128, dtype=np.float32)

    tri = np.tril(np.ones((128, 128), np.float32))
    causal = np.where(tri > 0, 0.0, np.float32(NEGBIG)).astype(np.float32)

    in_maps, own_rows_all = [], []
    Bb = x.shape[0]
    for core in range(2 * Bb):
        bi, p = core // 2, core % 2
        gtiles = np.arange(c.OT) * 2 + p
        own_rows = (gtiles[:, None] * 128 + np.arange(128)[None, :]).reshape(-1)
        own_rows_all.append((bi, own_rows))
        if p == 0:
            maskp = np.concatenate([causal, np.full((128, 128), NEGBIG, np.float32)], axis=0)
        else:
            maskp = np.concatenate([np.zeros((128, 128), np.float32), causal], axis=0)
        in_maps.append(dict(
            x_b=np.ascontiguousarray(x[bi]),
            x_own=np.ascontiguousarray(x[bi][own_rows]),
            wq=wq_p, wk=wk_p, wv=Wv, wo=Wo, w1t=w1t, w2t=w2t,
            b1r=b1r, gr=gr, brr=brr, b2r=b2r,
            cosk=cosk, sink=sink,
            cosq=np.ascontiguousarray(cosk[:, own_rows]),
            sinq=np.ascontiguousarray(sink[:, own_rows]),
            maskp=np.ascontiguousarray(maskp), ident_in=ident,
        ))
    return in_maps, own_rows_all


def kernel(**inputs):
    cfg = CFG()
    in_maps, own_rows_all = host_prepare(cfg, inputs)
    nc = Bld(cfg).build()
    res = run_bass_kernel_spmd(nc, in_maps, core_ids=list(range(len(in_maps))))
    x = inputs["L0_attn_q"]
    out = np.empty((x.shape[0], cfg.KV, cfg.D), dtype=np.float32)
    for core, r in enumerate(res.results):
        bi, own_rows = own_rows_all[core]
        out[bi][own_rows] = r["y"]
    return out
